# version 08 (final)
# baseline (speedup 1.0000x reference)
import numpy as np
import ml_dtypes
import concourse.bass as bass
import concourse.mybir as mybir
from concourse.bass_utils import run_bass_kernel_spmd

F32 = mybir.dt.float32
BF16 = mybir.dt.bfloat16
I32 = mybir.dt.int32
AFT = mybir.ActivationFunctionType
ALU = mybir.AluOpType
BF = ml_dtypes.bfloat16

D = 1024
S = 256
HF = 4096
DC = D // 128
SC = S // 128
FC = HF // 128
NCORE = 8
RSQRT_MAGIC = 0x5F3759DF


def build_program(W=512, NG=8):
    G = 4 * W
    T = NG * G
    H = W // 2
    NFP = HF // 256

    nc = bass.Bass("TRN2", target_bir_lowering=False)

    xT = nc.dram_tensor("xT", [D, T], F32, kind="ExternalInput")
    xqT = nc.dram_tensor("xqT", [D, NG * W], F32, kind="ExternalInput")
    W_in = nc.dram_tensor("W_in", [D, S], F32, kind="ExternalInput")
    Bm = nc.dram_tensor("Bm", [S, S], F32, kind="ExternalInput")
    Amat = nc.dram_tensor("Amat", [S, S], F32, kind="ExternalInput")
    CT = nc.dram_tensor("CT", [S, D], F32, kind="ExternalInput")
    DT = nc.dram_tensor("DT", [S, D], F32, kind="ExternalInput")
    Wg = nc.dram_tensor("Wg", [D, D], F32, kind="ExternalInput")
    Wo = nc.dram_tensor("Wo", [D, D], F32, kind="ExternalInput")
    w1 = nc.dram_tensor("w1", [HF, D], F32, kind="ExternalInput")
    w1T = nc.dram_tensor("w1T", [D, HF], F32, kind="ExternalInput")
    w2 = nc.dram_tensor("w2", [D, HF], F32, kind="ExternalInput")
    w2T = nc.dram_tensor("w2T", [HF, D], F32, kind="ExternalInput")
    b_in = nc.dram_tensor("b_in", [S], F32, kind="ExternalInput")
    b_gate = nc.dram_tensor("b_gate", [D], F32, kind="ExternalInput")
    b_out = nc.dram_tensor("b_out", [D], F32, kind="ExternalInput")
    b1 = nc.dram_tensor("b1", [HF], F32, kind="ExternalInput")
    b2 = nc.dram_tensor("b2", [D], F32, kind="ExternalInput")
    ident = nc.dram_tensor("ident", [128, 128], BF16, kind="ExternalInput")
    onesm = nc.dram_tensor("onesm", [128, 128], BF16, kind="ExternalInput")
    cpar = nc.dram_tensor("cpar", [1, 8], I32, kind="ExternalInput")
    out = nc.dram_tensor("out", [D, NG * W], F32, kind="ExternalOutput")

    bw1d = nc.dram_tensor("bw1d", [NFP, D, 256], BF16, kind="Internal")
    bw2d = nc.dram_tensor("bw2d", [HF, D], BF16, kind="Internal")
    hscr = nc.dram_tensor("hscr", [128, 2, G], BF16, kind="Internal")
    wg_d = nc.dram_tensor("wg_d", [DC, 128, DC * 128], BF16, kind="Internal")
    wo_d = nc.dram_tensor("wo_d", [DC, 128, DC * 128], BF16, kind="Internal")

    sb = nc.alloc_sbuf_tensor
    Win_bf = sb("Win_bf", [128, DC, S], BF16)
    CT_bf = sb("CT_bf", [128, SC, D], BF16)
    DT_bf = sb("DT_bf", [128, SC, D], BF16)
    A_bf = sb("A_bf", [128, SC, S], BF16)
    Bm_bf = sb("Bm_bf", [128, SC, S], BF16)
    I_bf = sb("I_bf", [128, 128], BF16)
    ones_bf = sb("ones_bf", [128, 128], BF16)

    b_in2 = sb("b_in2", [128, SC], F32)
    bg8 = sb("bg8", [128, DC], F32)
    bo8 = sb("bo8", [128, DC], F32)
    b2_8 = sb("b2_8", [128, DC], F32)
    b1_32 = sb("b1_32", [128, FC], F32)
    a1n = sb("a1n", [128, FC], F32)
    a2h = sb("a2h", [128, DC], F32)
    half_col = sb("half_col", [128, 1], F32)
    bgh8 = sb("bgh8", [128, DC], F32)
    cpar_sb = sb("cpar_sb", [1, 8], I32)
    sinit = sb("sinit", [128, 2], BF16)

    ub_ring = sb("ub_ring", [128, 2, 2 * G], BF16)
    h_ring = sb("h_ring", [128, 2, 2 * G], BF16)

    PW = max(DC * W, HF)
    F32P0 = sb("F32P0", [128, DC * W], F32)
    F32P1 = sb("F32P1", [128, PW], F32)
    F32P2 = sb("F32P2", [128, PW], F32)
    GBF = sb("GBF", [128, DC, W], BF16)
    GCVT = sb("GCVT", [128, W], F32)
    MIXT = sb("MIXT", [128, W], F32)
    XQBF = sb("XQBF", [128, DC, W], BF16)
    RBF0 = sb("RBF0", [128, PW], BF16)
    X1BF = sb("X1BF", [128, DC, W], BF16)
    YBF = sb("YBF", [128, DC, W], BF16)
    H2BF = sb("H2BF", [128, FC, H], BF16)
    LNP = sb("LNP", [128, 4, W], F32)
    USTG = sb("USTG", [128, SC, W], BF16)
    MYU = sb("MYU", [128, SC, W], BF16)
    MYH = sb("MYH", [128, SC, W], BF16)
    XSL = sb("XSL", [128, DC, W], BF16)
    BW1 = sb("BW1", [128, 2, DC, 256], BF16)
    BW2 = sb("BW2", [128, 2, 2, D], BF16)
    BWGO = sb("BWGO", [128, 2, DC, 128], BF16)

    ps_scan = [nc.alloc_psum_tensor("pss0", [128, 2], F32),
               nc.alloc_psum_tensor("pss1", [128, 2], F32)]
    PSD = [nc.alloc_psum_tensor(f"psd{i}", [128, W], F32) for i in range(4)]
    PST = [nc.alloc_psum_tensor("pst0", [128, W], F32),
           nc.alloc_psum_tensor("pst1", [128, W], F32)]

    sem_names = ["LDW", "TS", "SS", "GX", "GXQ", "GHX", "SYQ", "OUT",
                 "VDS", "SDS", "PEB2", "GB2"]
    sem_names += [f"MMS{i}" for i in range(6)]
    semh = {}
    semc = {n: 0 for n in sem_names}
    sem_ctx = []
    for n in sem_names:
        cm = nc.semaphore(n)
        h = cm.__enter__()
        sem_ctx.append(cm)
        semh[n] = h

    EL = {k: [] for k in ("tensor", "scalar", "vector", "sync", "gpsimd")}

    def em(engine, fn):
        EL[engine].append(fn)

    def bump(name, k=1):
        semc[name] += k
        return semc[name]

    NONE = ("VDS", 0)

    bank_rr = [0]
    bank_guard = [NONE] * 6
    bank_mms = [0] * 6

    def bank_tensor(i):
        return PSD[i] if i < 4 else PST[i - 4]

    def acquire_bank():
        i = bank_rr[0] % 4
        bank_rr[0] += 1
        return i, bank_guard[i]

    pe_weave = []
    sc_weave = []
    pe_pushed = [0]
    vds_sdsreq = {0: 0}
    dve_sds_max = [0]

    def wait_sds_req(waits):
        req = 0
        for (sn, sv) in waits:
            if sv <= 0:
                continue
            if sn == "SDS":
                req = max(req, sv)
            elif sn == "VDS":
                req = max(req, vds_sdsreq.get(sv, dve_sds_max[0]))
        return req

    def pe_push(fn, req_sds=0):
        pe_weave.append((req_sds, fn))
        pe_pushed[0] += 1

    def pe_pause(n):
        pe_weave.append(("P", n))

    def sc_push(fn):
        sc_weave.append((pe_pushed[0], semc["SDS"] + 1, fn))

    def plan_mm_group(mms, waits=(), inc_name=None, extra_inc=None,
                      per_item=2):
        chunks = [mms[i:i + per_item] for i in range(0, len(mms), per_item)]
        n = len(chunks)
        base = 0
        grp_req = wait_sds_req(waits)
        for ci, ch in enumerate(chunks):
            ww = [(sn, sv) for (sn, sv) in waits if sv > 0] if ci == 0 else []
            inc = inc_name if ci == n - 1 else None
            ei = []
            for j in range(len(ch)):
                if extra_inc is not None and base + j == extra_inc[0]:
                    ei.append((j, extra_inc[1]))

            def f(eng, ch=ch, ww=ww, inc=inc, ei=ei):
                for (sn, sv) in ww[1:]:
                    eng.wait_ge(semh[sn], sv)
                for j, (o, l, r, st, sp) in enumerate(ch):
                    m = eng.matmul(o(), l(), r(), start=st, stop=sp)
                    if j == 0 and ww:
                        m._wait_ge(semh[ww[0][0]], ww[0][1])
                    carried = False
                    for (jj, sname) in ei:
                        if jj == j:
                            m.then_inc(semh[sname], 1)
                            carried = True
                    if (not carried) and j == len(ch) - 1 and inc is not None:
                        m.then_inc(semh[inc], 1)
            pe_push(f, grp_req if ci == 0 else 0)
            base += len(ch)

    def mm_bank(mms, waits, bi, per_item=2):
        bump("MMS%d" % bi)
        bank_mms[bi] += 1
        plan_mm_group(mms, waits=waits, inc_name="MMS%d" % bi,
                      per_item=per_item)
        return ("MMS%d" % bi, bank_mms[bi])

    def dve_item(ops, waits=(), pad=False):
        if pad:
            ops = list(ops) + [lambda v: v.tensor_copy(LNP[:, 2, 0:1], half_col[:])]
        vals = [bump("VDS") for _ in ops]
        wl = [(sn, sv) for (sn, sv) in waits if sv > 0]
        dve_sds_max[0] = max(dve_sds_max[0], wait_sds_req(waits))
        for vv_ in vals:
            vds_sdsreq[vv_] = dve_sds_max[0]

        def f(v, ops=ops, wl=wl, vals=vals):
            for (sn, sv) in wl:
                v.wait_ge(semh[sn], sv)
            for i, op in enumerate(ops):
                if vals[i] - 1 > 0:
                    v.wait_ge(semh["VDS"], vals[i] - 1)
                op(v).then_inc(semh["VDS"], 1)
        em("vector", f)
        return vals

    def g_startup(g):
        c = 0
        loads = [
            (Win_bf[:], bass.AP(W_in, 0, [[S, 128], [128 * S, DC], [1, S]]), {}),
            (CT_bf[:], bass.AP(CT, 0, [[D, 128], [128 * D, SC], [1, D]]), {}),
            (DT_bf[:], bass.AP(DT, 0, [[D, 128], [128 * D, SC], [1, D]]), {}),
            (A_bf[:], bass.AP(Amat, 0, [[S, 128], [128 * S, SC], [1, S]]), {}),
            (Bm_bf[:], bass.AP(Bm, 0, [[S, 128], [128 * S, SC], [1, S]]), {}),
            (I_bf[:], ident[:], {}),
            (ones_bf[:], onesm[:], {}),
            (b_in2[:], bass.AP(b_in, 0, [[1, 128], [128, SC]]),
             dict(allow_slow_non_contiguous=True)),
            (bg8[:], bass.AP(b_gate, 0, [[1, 128], [128, DC]]),
             dict(allow_slow_non_contiguous=True)),
            (bo8[:], bass.AP(b_out, 0, [[1, 128], [128, DC]]),
             dict(allow_slow_non_contiguous=True)),
            (b2_8[:], bass.AP(b2, 0, [[1, 128], [128, DC]]),
             dict(allow_slow_non_contiguous=True)),
            (b1_32[:], bass.AP(b1, 0, [[1, 128], [128, FC]]),
             dict(allow_slow_non_contiguous=True)),
            (cpar_sb[:], cpar[:], {}),
        ]
        for (o, i, kw) in loads:
            if c > 0:
                g.wait_ge(semh["LDW"], c)
            g.dma_start(out=o, in_=i, **kw).then_inc(semh["LDW"], 16)
            c += 16

    NLDW = 13
    semc["LDW"] = 16 * NLDW
    em("gpsimd", g_startup)

    assert FC * H >= D

    def g_wcast(g):
        c = 0
        for (srcm, dstm) in ((Wg, wg_d), (Wo, wo_d)):
            for k in range(DC):
                bounce = H2BF[:, 0:D // H, :] if H <= D else H2BF[:, 0:1, 0:D]
                if c > 0:
                    g.wait_ge(semh["GXQ"], c)
                g.dma_start(out=bounce,
                            in_=bass.AP(srcm, k * 128 * D, [[D, 128], [1, D]])
                            ).then_inc(semh["GXQ"], 16)
                c += 16
                g.wait_ge(semh["GXQ"], c)
                g.dma_start(out=bass.AP(dstm, k * 128,
                                        [[DC * 128, 128], [128 * DC * 128, DC], [1, 128]]),
                            in_=bounce, allow_slow_non_contiguous=True
                            ).then_inc(semh["GXQ"], 16)
                c += 16
    em("gpsimd", g_wcast)
    WCAST_V = bump("GXQ", 16 * 2 * DC * 2)

    def f_reg(g):
        g.wait_ge(semh["LDW"], 16 * NLDW)
        g.reg_load(build_program._roff, cpar_sb[0:1, 0:1])
    em("gpsimd", f_reg)

    stageA, stageB = F32P1, F32P2
    sgA = RBF0
    stage_guard = [NONE, NONE]
    sg_guard = [NONE]

    def boot_load(srcm, off, width, buf):
        guard = stage_guard[buf]
        pre = semc["SYQ"]

        def f_load(s, srcm=srcm, off=off, width=width, buf=buf, guard=guard,
                   pre=pre):
            st = stageA if buf == 0 else stageB
            if guard[1] > 0:
                s.wait_ge(semh[guard[0]], guard[1])
            if pre > 0:
                s.wait_ge(semh["SYQ"], pre)
            s.dma_start(out=st[:, 0:width],
                        in_=bass.AP(srcm, off, [[width, 128], [1, width]])
                        ).then_inc(semh["SYQ"], 16)
        em("sync", f_load)
        return bump("SYQ", 16)

    def boot_sign_store(width, dram_ap_fn, sbuf_src_fn, buf, lv):
        sgg = sg_guard[0]

        def f_sign(sc_, width=width, buf=buf, lv=lv, sgg=sgg):
            st = stageA if buf == 0 else stageB
            sc_.wait_ge(semh["SYQ"], lv)
            if sgg[1] > 0:
                sc_.wait_ge(semh[sgg[0]], sgg[1])
            sc_.activation(sgA[:, 0:width], st[:, 0:width], AFT.Sign
                           ).then_inc(semh["SDS"], 1)
        em("scalar", f_sign)
        sgv = bump("SDS")
        stage_guard[buf] = ("SDS", sgv)
        ghx_pre = semc["GHX"]

        def f_store(g, sgv=sgv, dram_ap_fn=dram_ap_fn, sbuf_src_fn=sbuf_src_fn,
                    ghx_pre=ghx_pre):
            g.wait_ge(semh["SDS"], sgv)
            if ghx_pre > 0:
                g.wait_ge(semh["GHX"], ghx_pre)
            g.dma_start(out=dram_ap_fn(), in_=sbuf_src_fn(sgA)
                        ).then_inc(semh["GHX"], 16)
        em("gpsimd", f_store)
        gv = bump("GHX", 16)
        sg_guard[0] = ("GHX", gv)

    for k in range(DC):
        buf = k % 2
        lv = boot_load(w1T, k * 128 * HF, HF, buf)
        boot_sign_store(
            HF,
            (lambda kk=k: bass.AP(bw1d, kk * 128 * 256,
                                  [[256, 128], [D * 256, NFP], [1, 256]])),
            (lambda sg: sg[:, 0:HF].rearrange("p (a b) -> p a b", a=NFP)),
            buf, lv)
    for k in range(FC):
        buf = k % 2
        lv = boot_load(w2T, k * 128 * D, D, buf)
        boot_sign_store(
            D,
            (lambda kk=k: bass.AP(bw2d, kk * 128 * D, [[D, 128], [1, D]])),
            (lambda sg: sg[:, 0:D]),
            buf, lv)

    for k in range(FC):
        buf = k % 2
        lv = boot_load(w1, k * 128 * D, D, buf)
        vv = dve_item(
            [lambda v, kk=k, buf=buf: v.reduce_sum(
                a1n[:, kk:kk + 1],
                (stageA if buf == 0 else stageB)[:, 0:D],
                mybir.AxisListType.X, apply_absolute_value=True)],
            waits=[("SYQ", lv)])
        stage_guard[buf] = ("VDS", vv[-1])
    for k in range(DC):
        buf = k % 2
        lv = boot_load(w2, k * 128 * HF, HF, buf)
        vv = dve_item(
            [lambda v, kk=k, buf=buf: v.reduce_sum(
                a2h[:, kk:kk + 1],
                (stageA if buf == 0 else stageB)[:, 0:HF],
                mybir.AxisListType.X, apply_absolute_value=True)],
            waits=[("SYQ", lv)])
        stage_guard[buf] = ("VDS", vv[-1])

    ALPHA_V = semc["VDS"]
    cv = dve_item(
        [lambda v: v.memset(half_col[:], 0.5),
         lambda v: v.tensor_scalar_mul(bgh8[:], bg8[:], 0.5),
         lambda v: v.tensor_scalar_mul(a1n[:], a1n[:], 1.0 / D),
         lambda v: v.tensor_scalar_mul(a2h[:], a2h[:], 1.0 / HF)],
        waits=[("LDW", 16 * NLDW), ("VDS", ALPHA_V)])
    CONSTS_V = cv[-1]
    BOOT_GHX = semc["GHX"]

    def f_sstart(sc_):
        sc_.wait_ge(semh["LDW"], 16 * NLDW)
        sc_.wait_ge(semh["VDS"], CONSTS_V)
        sc_.preload_act_table(AFT.Gelu)
        sc_.memzero(sinit[:]).then_inc(semh["SS"], 1)
    em("scalar", f_sstart)
    bump("SS")

    xsl_count = [0]
    xsl_guard = [NONE]
    xqbf_guard = [NONE]
    myh_guard = [NONE]
    gbf_guard = [NONE]
    h2_guard = [NONE]
    bw1_count = [0]
    bw1_guard = {}
    bw2_count = [0]
    bw2_guard = {}
    wgo_count = [0]
    wgo_guard = {}
    outd_count = [0]
    ext_done = {}

    PI1 = 1 if W >= 384 else 2

    def plan_uub(g_idx):
        slot = g_idx % 2
        for tt in range(4):
            n = xsl_count[0]
            guard = xsl_guard[0]
            gx_pre = 16 * n

            def f(g, g_idx=g_idx, tt=tt, guard=guard, gx_pre=gx_pre):
                if guard[1] > 0:
                    g.wait_ge(semh[guard[0]], guard[1])
                if gx_pre > 0:
                    g.wait_ge(semh["GX"], gx_pre)
                c0 = g_idx * G + tt * W
                g.dma_start(out=XSL[:],
                            in_=bass.AP(xT, c0, [[T, 128], [128 * T, DC], [1, W]])
                            ).then_inc(semh["GX"], 16)
            em("gpsimd", f)
            xsl_count[0] += 1
            gxv = bump("GX", 16)
            slab = g_idx * 4 + tt
            u_done = None
            for m in range(SC):
                bi, bg_ = acquire_bank()
                mms = []
                for k in range(DC):
                    mms.append((
                        (lambda bi=bi: bank_tensor(bi)[:, 0:W]),
                        (lambda k=k, m=m: Win_bf[:, k, m * 128:(m + 1) * 128]),
                        (lambda k=k: XSL[:, k, :]),
                        k == 0, k == DC - 1))
                mg = mm_bank(mms, [bg_, ("GX", gxv), ("LDW", 16 * NLDW)], bi, per_item=PI1)
                u_done = mg
                vv = dve_item(
                    [lambda v, bi=bi, m=m: v.tensor_scalar_add(
                        USTG[:, m, :], bank_tensor(bi)[:, 0:W],
                        b_in2[:, m:m + 1])],
                    waits=[mg], pad=True)
                bank_guard[bi] = ("VDS", vv[-1])
            xsl_guard[0] = u_done
            ustg_v = semc["VDS"]
            for j in range(SC):
                bi, bg_ = acquire_bank()
                mms = []
                for k in range(SC):
                    mms.append((
                        (lambda bi=bi: bank_tensor(bi)[:, 0:W]),
                        (lambda k=k, j=j: Bm_bf[:, k, j * 128:(j + 1) * 128]),
                        (lambda k=k: USTG[:, k, :]),
                        k == 0, k == SC - 1))
                mg = mm_bank(mms, [bg_, ("VDS", ustg_v)], bi, per_item=PI1)
                vv = dve_item(
                    [lambda v, bi=bi, j=j, slot=slot, tt=tt: v.tensor_copy(
                        ub_ring[:, j, slot * G + tt * W: slot * G + (tt + 1) * W],
                        bank_tensor(bi)[:, 0:W])],
                    waits=[mg])
                bank_guard[bi] = ("VDS", vv[-1])
        return semc["VDS"]

    def plan_layernorm(src, dst, dst_bf, src_v):
        ops = []
        for dcn in range(DC):
            ops.append(lambda v, dcn=dcn: v.tensor_copy(
                RBF0[:, dcn * W:(dcn + 1) * W], src[:, dcn * W:(dcn + 1) * W]))
        for dcn in range(DC):
            ops.append(lambda v, dcn=dcn: v.tensor_mul(
                YBF[:, dcn, :], RBF0[:, dcn * W:(dcn + 1) * W],
                RBF0[:, dcn * W:(dcn + 1) * W]))
        cast_v = dve_item(ops, waits=[("VDS", src_v)], pad=True)[-1]

        stat_g = []
        for q in range(2):
            bi = 4 + q
            bg_ = bank_guard[bi]
            mms = []
            for k in range(DC):
                if q == 0:
                    rhs_fn = (lambda k=k: RBF0[:, k * W:(k + 1) * W])
                else:
                    rhs_fn = (lambda k=k: YBF[:, k, :])
                mms.append((
                    (lambda bi=bi: bank_tensor(bi)[:, 0:W]),
                    (lambda: ones_bf[:]),
                    rhs_fn, k == 0, k == DC - 1))
            stat_g.append(mm_bank(mms, [bg_, ("VDS", cast_v)], bi, per_item=PI1))

        m = LNP[:, 0, :]
        qq = LNP[:, 1, :]
        t1 = LNP[:, 2, :]
        r = LNP[:, 3, :]
        ops = [
            lambda v: v.tensor_scalar_mul(m, PST[0][:, 0:W], 1.0 / D),
            lambda v: v.tensor_scalar_mul(qq, PST[1][:, 0:W], 1.0 / D),
            lambda v: v.tensor_mul(t1, m, m),
            lambda v: v.tensor_sub(qq, qq, t1),
            lambda v: v.tensor_scalar_add(qq, qq, 1e-5),
            lambda v: v.tensor_single_scalar(r[:].bitcast(I32), qq[:].bitcast(I32),
                                             1, ALU.logical_shift_right),
            lambda v: v.tensor_single_scalar(r[:].bitcast(I32), r[:].bitcast(I32),
                                             -1, ALU.bitwise_xor),
            lambda v: v.tensor_single_scalar(r[:].bitcast(I32), r[:].bitcast(I32),
                                             RSQRT_MAGIC + 1, ALU.add),
        ]
        for _ in range(2):
            ops += [
                lambda v: v.tensor_mul(t1, r, r),
                lambda v: v.tensor_mul(t1, t1, qq),
                lambda v: v.tensor_scalar(t1, t1, -0.5, 1.5, ALU.mult, ALU.add),
                lambda v: v.tensor_mul(r, r, t1),
            ]
        ops.append(lambda v: v.tensor_mul(m, m, r))
        for dcn in range(DC):
            sl = slice(dcn * W, (dcn + 1) * W)
            ops.append(lambda v, sl=sl: v.tensor_mul(dst[:, sl], src[:, sl], r))
            ops.append(lambda v, sl=sl: v.tensor_sub(dst[:, sl], dst[:, sl], m))
        if dst_bf is not None:
            for dcn in range(DC):
                ops.append(lambda v, dcn=dcn: v.tensor_copy(
                    dst_bf[:, dcn, :], dst[:, dcn * W:(dcn + 1) * W]))
        lvals = dve_item(ops, waits=stat_g, pad=True)
        bank_guard[4] = ("VDS", lvals[0])
        bank_guard[5] = ("VDS", lvals[1])
        return lvals[-1]

    def plan_ffn(s, th, x1_v):
        gelu_byf = {}
        for fp in range(NFP):
            buf = bw1_count[0] % 2
            guard = bw1_guard.get(buf, NONE)
            qeng, qsem = (("sync", "SYQ") if buf == 0 else ("gpsimd", "GB2"))
            q_pre = semc[qsem]

            def f_bw1(sy, fp=fp, buf=buf, guard=guard, q_pre=q_pre, qsem=qsem):
                if q_pre > 0:
                    sy.wait_ge(semh[qsem], q_pre)
                if guard[1] > 0:
                    sy.wait_ge(semh[guard[0]], guard[1])
                sy.dma_start(out=BW1[:, buf, :, :],
                             in_=bass.AP(bw1d, fp * D * 256,
                                         [[256, 128], [128 * 256, DC], [1, 256]])
                             ).then_inc(semh[qsem], 16)
            em(qeng, f_bw1)
            bw1_count[0] += 1
            slab_v = (qsem, bump(qsem, 16))

            for hh in range(2):
                f = 2 * fp + hh
                bi, bg_ = acquire_bank()
                mms = []
                for k in range(DC):
                    mms.append((
                        (lambda bi=bi: bank_tensor(bi)[:, 0:H]),
                        (lambda k=k, hh=hh, buf=buf: BW1[:, buf, k, hh * 128:(hh + 1) * 128]),
                        (lambda k=k, th=th: X1BF[:, k, th * H:(th + 1) * H]),
                        k == 0, k == DC - 1))
                mg = mm_bank(mms, [bg_, slab_v, ("VDS", x1_v)], bi)
                if hh == 1:
                    bw1_guard[buf] = mg

                h2g = h2_guard[0]

                def fsc(sc_, bi=bi, f=f, mg=mg, h2g=h2g):
                    sc_.wait_ge(semh[mg[0]], mg[1])
                    if h2g[1] > 0:
                        sc_.wait_ge(semh[h2g[0]], h2g[1])
                    sc_.activation(H2BF[:, f, :], bank_tensor(bi)[:, 0:H],
                                   AFT.Gelu, bias=b1_32[:, f:f + 1],
                                   scale=a1n[:, f:f + 1]).then_inc(semh["SDS"], 1)
                sc_push(fsc)
                gelu_v = bump("SDS")
                bank_guard[bi] = ("SDS", gelu_v)
                gelu_byf[f] = gelu_v
        h2_v = ("SDS", semc["SDS"])

        for pass_ in range(2):
            dcs = [pass_ * 4 + i for i in range(4)]
            bank_waits = [bank_guard[i] for i in range(4)]
            for f in range(FC):
                fp, half = f // 2, f % 2
                if half == 0:
                    buf = bw2_count[0] % 2
                    guard = bw2_guard.get(buf, NONE)
                    qeng, qsem = (("gpsimd", "GB2") if buf == 0 else ("sync", "SYQ"))
                    q_pre = semc[qsem]

                    def f_bw2(g, fp=fp, buf=buf, guard=guard, q_pre=q_pre, qsem=qsem):
                        if q_pre > 0:
                            g.wait_ge(semh[qsem], q_pre)
                        if guard[1] > 0:
                            g.wait_ge(semh[guard[0]], guard[1])
                        g.dma_start(out=BW2[:, buf, :, :],
                                    in_=bass.AP(bw2d, fp * 256 * D,
                                                [[D, 128], [128 * D, 2], [1, D]])
                                    ).then_inc(semh[qsem], 16)
                    em(qeng, f_bw2)
                    bw2_count[0] += 1
                    slab_v = (qsem, bump(qsem, 16))
                    cur_bw2 = (buf, slab_v)
                buf, slab_v = cur_bw2

                mms = []
                for i, dcn in enumerate(dcs):
                    mms.append((
                        (lambda i=i: bank_tensor(i)[:, 0:H]),
                        (lambda dcn=dcn, buf=buf, half=half: BW2[:, buf, half, dcn * 128:(dcn + 1) * 128]),
                        (lambda f=f, th=th: H2BF[:, f, :]),
                        f == 0, f == FC - 1))
                ww = [slab_v, ("SDS", gelu_byf[f])]
                if f == 0:
                    ww = ww + bank_waits
                if f < FC - 1:
                    if half == 1:
                        plan_mm_group(mms, waits=ww, inc_name=None,
                                      extra_inc=(3, "PEB2"))
                        bw2_guard[buf] = ("PEB2", bump("PEB2"))
                    else:
                        plan_mm_group(mms, waits=ww, inc_name=None)
                else:
                    for i in range(4):
                        bump("MMS%d" % i)
                        bank_mms[i] += 1
                    req = wait_sds_req(ww)

                    def f_last(eng, mms=mms, ww=ww):
                        wl = [(a, b) for (a, b) in ww if b > 0]
                        for (sn, sv) in wl[1:]:
                            eng.wait_ge(semh[sn], sv)
                        for i, (o, l, r, st, sp) in enumerate(mms):
                            m = eng.matmul(o(), l(), r(), start=st, stop=sp)
                            if i == 0 and wl:
                                m._wait_ge(semh[wl[0][0]], wl[0][1])
                            m.then_inc(semh["MMS%d" % i], 1)
                    pe_push(f_last, req)
                    bw2_guard[buf] = ("MMS3", bank_mms[3])
            pass_g = [("MMS%d" % i, bank_mms[i]) for i in range(4)]
            if pass_ == 1:
                h2_guard[0] = ("MMS3", bank_mms[3])

            for i, dcn in enumerate(dcs):
                mg = pass_g[i]
                vv = dve_item(
                    [lambda v, i=i, dcn=dcn, th=th: v.tensor_scalar(
                        F32P1[:, dcn * W + th * H: dcn * W + (th + 1) * H],
                        bank_tensor(i)[:, 0:H],
                        a2h[:, dcn:dcn + 1], b2_8[:, dcn:dcn + 1],
                        ALU.mult, ALU.add),
                     lambda v, dcn=dcn, th=th: v.tensor_add(
                        F32P1[:, dcn * W + th * H: dcn * W + (th + 1) * H],
                        F32P1[:, dcn * W + th * H: dcn * W + (th + 1) * H],
                        F32P2[:, dcn * W + th * H: dcn * W + (th + 1) * H])],
                    waits=[mg])
                bank_guard[i] = ("VDS", vv[0])

    def plan_ds_io(s):
        slot = s % 2
        s_endv = 1 + (s + 1) * G
        ghx_pre = semc["GHX"]

        def f_ext(g, slot=slot, s_endv=s_endv, ghx_pre=ghx_pre):
            g.wait_ge(semh["SS"], s_endv)
            if ghx_pre > 0:
                g.wait_ge(semh["GHX"], ghx_pre)
            g.dma_start(out=hscr[:], in_=h_ring[:, :, slot * G:(slot + 1) * G]
                        ).then_inc(semh["GHX"], 16)
        em("gpsimd", f_ext)
        v1 = bump("GHX", 16)
        ext_done[s] = v1
        mhg = myh_guard[0]

        def f_ext2(g, v1=v1, mhg=mhg):
            g.wait_ge(semh["GHX"], v1)
            if mhg[1] > 0:
                g.wait_ge(semh[mhg[0]], mhg[1])
            g.dma_start(out=MYH[:],
                        in_=bass.AP(hscr, build_program._roff,
                                    [[2 * G, 128], [G, 2], [1, W]])
                        ).then_inc(semh["GHX"], 16)
        em("gpsimd", f_ext2)
        myh_v = bump("GHX", 16)

        syq_pre = semc["SYQ"]
        need_out = 16 * outd_count[0]

        def f_xq32(sy, s=s, need_out=need_out, syq_pre=syq_pre):
            if s == 0:
                sy.wait_ge(semh["GHX"], BOOT_GHX)
                sy.wait_ge(semh["GXQ"], WCAST_V)
            if need_out > 0:
                sy.wait_ge(semh["OUT"], need_out)
            if syq_pre > 0:
                sy.wait_ge(semh["SYQ"], syq_pre)
            sy.dma_start(out=F32P0[:],
                         in_=bass.AP(xqT, s * W, [[NG * W, 128], [128 * NG * W, DC], [1, W]])
                         ).then_inc(semh["SYQ"], 16)
        em("sync", f_xq32)
        xq32_v = bump("SYQ", 16)

        xqg = xqbf_guard[0]
        gxq_pre = semc["GXQ"]

        def f_xqbf(g, s=s, xqg=xqg, gxq_pre=gxq_pre):
            if xqg[1] > 0:
                g.wait_ge(semh[xqg[0]], xqg[1])
            if gxq_pre > 0:
                g.wait_ge(semh["GXQ"], gxq_pre)
            g.dma_start(out=XQBF[:],
                        in_=bass.AP(xqT, s * W, [[NG * W, 128], [128 * NG * W, DC], [1, W]])
                        ).then_inc(semh["GXQ"], 16)
        em("gpsimd", f_xqbf)
        xqbf_v = bump("GXQ", 16)
        return myh_v, xq32_v, xqbf_v

    def plan_downstream(s, io):
        myh_v, xq32_v, xqbf_v = io

        for m in range(SC):
            bi, bg_ = acquire_bank()
            mms = []
            for k in range(DC):
                mms.append((
                    (lambda bi=bi: bank_tensor(bi)[:, 0:W]),
                    (lambda k=k, m=m: Win_bf[:, k, m * 128:(m + 1) * 128]),
                    (lambda k=k: XQBF[:, k, :]),
                    k == 0, k == DC - 1))
            mg = mm_bank(mms, [bg_, ("GXQ", xqbf_v)], bi, per_item=PI1)
            vv = dve_item(
                [lambda v, bi=bi, m=m: v.tensor_scalar_add(
                    MYU[:, m, :], bank_tensor(bi)[:, 0:W], b_in2[:, m:m + 1])],
                waits=[mg], pad=True)
            bank_guard[bi] = ("VDS", vv[-1])
        myu_v = semc["VDS"]

        sig_last = 0
        d2_last = NONE
        for e in range(DC):
            buf = wgo_count[0] % 2
            guard = wgo_guard.get(buf, NONE)
            syq_pre2 = semc["SYQ"]

            def f_wg(sy, e=e, buf=buf, guard=guard, syq_pre2=syq_pre2):
                if syq_pre2 > 0:
                    sy.wait_ge(semh["SYQ"], syq_pre2)
                if guard[1] > 0:
                    sy.wait_ge(semh[guard[0]], guard[1])
                sy.dma_start(out=BWGO[:, buf, :, :],
                             in_=bass.AP(wg_d, e * 128 * DC * 128,
                                         [[DC * 128, 128], [1, DC * 128]])
                             ).then_inc(semh["SYQ"], 16)
            em("sync", f_wg)
            wgo_count[0] += 1
            slab_v = bump("SYQ", 16)

            bi, bg_ = acquire_bank()
            mms = []
            for k in range(DC):
                mms.append((
                    (lambda bi=bi: bank_tensor(bi)[:, 0:W]),
                    (lambda k=k, buf=buf: BWGO[:, buf, k, :]),
                    (lambda k=k: XQBF[:, k, :]),
                    k == 0, k == DC - 1))
            mg = mm_bank(mms, [bg_, ("GXQ", xqbf_v), ("SYQ", slab_v)], bi, per_item=PI1)
            wgo_guard[buf] = mg
            d2_last = mg

            gbg = gbf_guard[0]

            def fsc(sc_, bi=bi, e=e, mg=mg, gbg=gbg):
                sc_.wait_ge(semh[mg[0]], mg[1])
                if gbg[1] > 0:
                    sc_.wait_ge(semh[gbg[0]], gbg[1])
                sc_.activation(GBF[:, e, :], bank_tensor(bi)[:, 0:W],
                               AFT.Tanh, bias=bgh8[:, e:e + 1],
                               scale=half_col[:]).then_inc(semh["SDS"], 1)
            sc_push(fsc)
            sv = bump("SDS")
            sig_last = sv
            bank_guard[bi] = ("SDS", sv)
        xqbf_guard[0] = d2_last
        g_done = sig_last

        d3_last = NONE
        mix_last = NONE
        for e in range(DC):
            bi, bg_ = acquire_bank()
            mms = []
            for k in range(SC):
                mms.append((
                    (lambda bi=bi: bank_tensor(bi)[:, 0:W]),
                    (lambda k=k, e=e: CT_bf[:, k, e * 128:(e + 1) * 128]),
                    (lambda k=k: MYH[:, k, :]),
                    k == 0, False))
            for k in range(SC):
                mms.append((
                    (lambda bi=bi: bank_tensor(bi)[:, 0:W]),
                    (lambda k=k, e=e: DT_bf[:, k, e * 128:(e + 1) * 128]),
                    (lambda k=k: MYU[:, k, :]),
                    False, k == SC - 1))
            mg = mm_bank(mms, [bg_, ("GHX", myh_v), ("VDS", myu_v)], bi, per_item=PI1)
            d3_last = mg
            vv = dve_item(
                [lambda v, e=e: v.tensor_scalar(
                    GCVT[:], GBF[:, e, :], 0.5, 0.5, ALU.mult, ALU.add),
                 lambda v, bi=bi, e=e: v.tensor_sub(
                    MIXT[:], bank_tensor(bi)[:, 0:W], F32P0[:, e * W:(e + 1) * W]),
                 lambda v: v.tensor_mul(MIXT[:], MIXT[:], GCVT[:]),
                 lambda v, e=e: v.tensor_add(
                    YBF[:, e, :], MIXT[:], F32P0[:, e * W:(e + 1) * W])],
                waits=[mg, ("SYQ", xq32_v), ("SDS", g_done)], pad=True)
            bank_guard[bi] = ("VDS", vv[1])
            mix_last = ("VDS", vv[-1])
        myh_guard[0] = d3_last
        gbf_guard[0] = mix_last
        ybf_v = mix_last[1]

        for e2 in range(DC):
            buf = wgo_count[0] % 2
            guard = wgo_guard.get(buf, NONE)
            syq_pre3 = semc["SYQ"]

            def f_wo(sy, e2=e2, buf=buf, guard=guard, syq_pre3=syq_pre3):
                if syq_pre3 > 0:
                    sy.wait_ge(semh["SYQ"], syq_pre3)
                if guard[1] > 0:
                    sy.wait_ge(semh[guard[0]], guard[1])
                sy.dma_start(out=BWGO[:, buf, :, :],
                             in_=bass.AP(wo_d, e2 * 128 * DC * 128,
                                         [[DC * 128, 128], [1, DC * 128]])
                             ).then_inc(semh["SYQ"], 16)
            em("sync", f_wo)
            wgo_count[0] += 1
            slab_v = bump("SYQ", 16)

            bi, bg_ = acquire_bank()
            mms = []
            for k in range(DC):
                mms.append((
                    (lambda bi=bi: bank_tensor(bi)[:, 0:W]),
                    (lambda k=k, buf=buf: BWGO[:, buf, k, :]),
                    (lambda k=k: YBF[:, k, :]),
                    k == 0, k == DC - 1))
            mg = mm_bank(mms, [bg_, ("VDS", ybf_v), ("SYQ", slab_v)], bi, per_item=PI1)
            wgo_guard[buf] = mg
            vv = dve_item(
                [lambda v, bi=bi, e2=e2: v.tensor_scalar_add(
                    F32P1[:, e2 * W:(e2 + 1) * W], bank_tensor(bi)[:, 0:W],
                    bo8[:, e2:e2 + 1]),
                 lambda v, e2=e2: v.tensor_add(
                    F32P1[:, e2 * W:(e2 + 1) * W], F32P1[:, e2 * W:(e2 + 1) * W],
                    F32P0[:, e2 * W:(e2 + 1) * W]),
                 lambda v, e2=e2: v.tensor_add(
                    F32P1[:, e2 * W:(e2 + 1) * W], F32P1[:, e2 * W:(e2 + 1) * W],
                    F32P0[:, e2 * W:(e2 + 1) * W])],
                waits=[mg])
            bank_guard[bi] = ("VDS", vv[0])
        res1_v = semc["VDS"]

        pe_pause(12)
        x1_v = plan_layernorm(F32P1, F32P2, X1BF, res1_v)
        pe_pause(40)
        for th in range(2):
            plan_ffn(s, th, x1_v)
        res2_v = semc["VDS"]
        pe_pause(44)
        x2_v = plan_layernorm(F32P1, F32P0, None, res2_v)

        out_pre = 16 * outd_count[0]

        def f_out(sy, s=s, x2_v=x2_v, out_pre=out_pre):
            sy.wait_ge(semh["VDS"], x2_v)
            if out_pre > 0:
                sy.wait_ge(semh["OUT"], out_pre)
            sy.dma_start(out=bass.AP(out, s * W, [[NG * W, 128], [128 * NG * W, DC], [1, W]]),
                         in_=F32P0[:]).then_inc(semh["OUT"], 16)
        em("sync", f_out)
        outd_count[0] += 1
        bump("OUT", 16)

    def plan_scan_step(g_idx, tl, ub_ready_v):
        t = g_idx * G + tl
        slot = g_idx % 2
        pslot = (g_idx - 1) % 2
        p = ps_scan[t % 2]
        ext_v = ext_done.get(g_idx - 2, 0) if tl == 0 else 0

        def f(eng, g_idx=g_idx, tl=tl, t=t, slot=slot, pslot=pslot, p=p,
              ub_ready_v=ub_ready_v, ext_v=ext_v):
            if ext_v > 0:
                eng.wait_ge(semh["GHX"], ext_v)
            m0 = eng.matmul(p[:, :], I_bf[:],
                            ub_ring[:, :, slot * G + tl: slot * G + tl + 1],
                            start=True, stop=False)
            if tl == 0 and ub_ready_v is not None:
                m0._wait_ge(semh["VDS"], ub_ready_v)
            if t == 0:
                rhs0, rhs1 = sinit[:, 0:1], sinit[:, 1:2]
            elif tl == 0:
                rhs0 = h_ring[:, 0, pslot * G + G - 1: pslot * G + G]
                rhs1 = h_ring[:, 1, pslot * G + G - 1: pslot * G + G]
            else:
                rhs0 = h_ring[:, 0, slot * G + tl - 1: slot * G + tl]
                rhs1 = h_ring[:, 1, slot * G + tl - 1: slot * G + tl]
            m1 = eng.matmul(p[:, 0:1], A_bf[:, 0, 0:128], rhs0, start=False, stop=False)
            m1._wait_ge(semh["SS"], t + 1)
            eng.matmul(p[:, 0:1], A_bf[:, 1, 0:128], rhs1, start=False, stop=False)
            eng.matmul(p[:, 1:2], A_bf[:, 0, 128:256], rhs0, start=False, stop=False)
            eng.matmul(p[:, 1:2], A_bf[:, 1, 128:256], rhs1, start=False,
                       stop=True).then_inc(semh["TS"], 1)
        bump("TS")

        def fsc(sc_, t=t, slot=slot, tl=tl, p=p):
            sc_.wait_ge(semh["TS"], t + 1)
            sc_.activation(h_ring[:, :, slot * G + tl: slot * G + tl + 1],
                           p[:, :], AFT.Tanh).then_inc(semh["SS"], 1)
        bump("SS")
        return f, fsc

    SC_PACE = 6
    pe_popped = [0]
    sds_placed = [semc["SDS"]]

    def sc_place_one():
        need_pe, sds_val, fn = sc_weave.pop(0)
        assert need_pe <= pe_popped[0], "sc item placed before its producer"
        EL["scalar"].append(fn)
        sds_placed[0] = sds_val

    def pe_drain_one():
        req, fn = pe_weave[0]
        if req == "P":
            if fn <= 1:
                pe_weave.pop(0)
            else:
                pe_weave[0] = ("P", fn - 1)
            return
        while req > sds_placed[0]:
            assert sc_weave, "PE item needs SDS that no queued sc item makes"
            sc_place_one()
        pe_weave.pop(0)
        em("tensor", fn)
        pe_popped[0] += 1

    def sc_try_drain():
        if sc_weave and sc_weave[0][0] <= pe_popped[0]:
            sc_place_one()

    ub_ready = {}
    ub_ready[0] = plan_uub(0)
    while pe_weave:
        pe_drain_one()

    for g_idx in range(NG):
        if g_idx >= 1:
            ds_io = plan_ds_io(g_idx - 1)
        if g_idx + 1 < NG:
            ub_ready[g_idx + 1] = plan_uub(g_idx + 1)
        if g_idx >= 1:
            plan_downstream(g_idx - 1, ds_io)
        for tl in range(G):
            f, fsc = plan_scan_step(g_idx, tl, ub_ready.get(g_idx))
            em("tensor", f)
            EL["scalar"].append(fsc)
            if pe_weave:
                pe_drain_one()
            if tl % SC_PACE == 0:
                sc_try_drain()
        while pe_weave:
            if pe_weave[0][0] == "P":
                pe_weave.pop(0)
                continue
            pe_drain_one()
        while sc_weave:
            sc_place_one()

    plan_downstream(NG - 1, plan_ds_io(NG - 1))
    while pe_weave:
        if pe_weave[0][0] == "P":
            pe_weave.pop(0)
            continue
        pe_drain_one()
    while sc_weave:
        sc_place_one()

    final_out = semc["OUT"]

    def f_fin(sy, final_out=final_out):
        sy.wait_ge(semh["OUT"], final_out)
    em("sync", f_fin)

    build_program._roff = nc.gpsimd.alloc_register("roff")
    with nc.Block() as block:

        @block.tensor
        def _(tensor):
            for f in EL["tensor"]:
                f(tensor)

        @block.scalar
        def _(scalar):
            for f in EL["scalar"]:
                f(scalar)

        @block.vector
        def _(vector):
            for f in EL["vector"]:
                f(vector)

        @block.sync
        def _(sync):
            for f in EL["sync"]:
                f(sync)

        @block.gpsimd
        def _(gpsimd):
            for f in EL["gpsimd"]:
                f(gpsimd)

    for cm in reversed(sem_ctx):
        cm.__exit__(None, None, None)
    return nc


def make_in_maps(inputs, W=512, NG=8):
    G = 4 * W
    x = np.asarray(inputs["L0_attn_x"], np.float32)
    B = x.shape[0]
    wts = {
        "W_in": np.ascontiguousarray(np.asarray(inputs["L0_attn_W_in"], dtype=np.float32)),
        "Bm": np.ascontiguousarray(np.asarray(inputs["L0_attn_Bm"], dtype=np.float32)),
        "Amat": np.ascontiguousarray(np.asarray(inputs["L0_attn_A"], dtype=np.float32)),
        "CT": np.ascontiguousarray(np.asarray(inputs["L0_attn_C"], dtype=np.float32).T),
        "DT": np.ascontiguousarray(np.asarray(inputs["L0_attn_D"], dtype=np.float32).T),
        "Wg": np.ascontiguousarray(np.asarray(inputs["L0_attn_W_gate"], dtype=np.float32)),
        "Wo": np.ascontiguousarray(np.asarray(inputs["L0_attn_W_out"], dtype=np.float32)),
        "w1": np.ascontiguousarray(np.asarray(inputs["L0_ffn_w1"], dtype=np.float32)),
        "w1T": np.ascontiguousarray(np.asarray(inputs["L0_ffn_w1"], dtype=np.float32).T),
        "w2": np.ascontiguousarray(np.asarray(inputs["L0_ffn_w2"], dtype=np.float32)),
        "w2T": np.ascontiguousarray(np.asarray(inputs["L0_ffn_w2"], dtype=np.float32).T),
        "b_in": np.ascontiguousarray(np.asarray(inputs["L0_attn_b_in"], dtype=np.float32)),
        "b_gate": np.ascontiguousarray(np.asarray(inputs["L0_attn_b_gate"], dtype=np.float32)),
        "b_out": np.ascontiguousarray(np.asarray(inputs["L0_attn_b_out"], dtype=np.float32)),
        "b1": np.ascontiguousarray(np.asarray(inputs["L0_ffn_b1"], dtype=np.float32)),
        "b2": np.ascontiguousarray(np.asarray(inputs["L0_ffn_b2"], dtype=np.float32)),
        "ident": np.eye(128, dtype=np.float32).astype(BF),
        "onesm": np.ones((128, 128), np.float32).astype(BF),
    }
    in_maps = []
    for c in range(NCORE):
        b = c // (NCORE // B)
        r = c % (NCORE // B)
        xb = x[b]
        xTb = np.ascontiguousarray(xb.T)
        idx = np.concatenate([np.arange(g * G + r * W, g * G + (r + 1) * W)
                              for g in range(NG)])
        xq = np.ascontiguousarray(xb[idx].T)
        cp = np.zeros((1, 8), np.int32)
        cp[0, 0] = r * W
        m = dict(wts)
        m["xT"] = xTb
        m["xqT"] = xq
        m["cpar"] = cp
        in_maps.append(m)
    return in_maps


def assemble(results, B, T, W=512, NG=8):
    G = 4 * W
    R = NCORE // B
    outx = np.empty((B, T, D), np.float32)
    for c in range(NCORE):
        b = c // R
        r = c % R
        oc = results[c]["out"]
        for g in range(NG):
            outx[b, g * G + r * W:g * G + (r + 1) * W, :] = oc[:, g * W:(g + 1) * W].T
    return outx


def kernel(**inputs):
    W, NG = 512, 8
    T = NG * 4 * W
    x = np.asarray(inputs["L0_attn_x"], np.float32)
    B = x.shape[0]
    nc = build_program(W=W, NG=NG)
    in_maps = make_in_maps(inputs, W=W, NG=NG)
    res = run_bass_kernel_spmd(nc, in_maps, core_ids=list(range(NCORE)))
    return assemble(res.results, B, T, W=W, NG=NG)


# version 10
# speedup vs baseline: 1.0353x; 1.0353x over previous
import numpy as np
import ml_dtypes
import concourse.bass as bass
import concourse.mybir as mybir
from concourse.bass_utils import run_bass_kernel_spmd

F32 = mybir.dt.float32
BF16 = mybir.dt.bfloat16
I32 = mybir.dt.int32
AFT = mybir.ActivationFunctionType
ALU = mybir.AluOpType
BF = ml_dtypes.bfloat16

D = 1024
S = 256
HF = 4096
DC = D // 128
SC = S // 128
FC = HF // 128
NCORE = 8
RSQRT_MAGIC = 0x5F3759DF


def build_program(W=512, NG=8):
    G = 4 * W
    T = NG * G
    H = W // 2
    NFP = HF // 256

    nc = bass.Bass("TRN2", target_bir_lowering=False)

    xT = nc.dram_tensor("xT", [D, T], F32, kind="ExternalInput")
    xqT = nc.dram_tensor("xqT", [D, NG * W], F32, kind="ExternalInput")
    W_in = nc.dram_tensor("W_in", [D, S], F32, kind="ExternalInput")
    Bm = nc.dram_tensor("Bm", [S, S], F32, kind="ExternalInput")
    Amat = nc.dram_tensor("Amat", [S, S], F32, kind="ExternalInput")
    CT = nc.dram_tensor("CT", [S, D], F32, kind="ExternalInput")
    DT = nc.dram_tensor("DT", [S, D], F32, kind="ExternalInput")
    Wg = nc.dram_tensor("Wg", [D, D], F32, kind="ExternalInput")
    Wo = nc.dram_tensor("Wo", [D, D], F32, kind="ExternalInput")
    w1 = nc.dram_tensor("w1", [HF, D], F32, kind="ExternalInput")
    w1T = nc.dram_tensor("w1T", [D, HF], F32, kind="ExternalInput")
    w2 = nc.dram_tensor("w2", [D, HF], F32, kind="ExternalInput")
    w2T = nc.dram_tensor("w2T", [HF, D], F32, kind="ExternalInput")
    b_in = nc.dram_tensor("b_in", [S], F32, kind="ExternalInput")
    b_gate = nc.dram_tensor("b_gate", [D], F32, kind="ExternalInput")
    b_out = nc.dram_tensor("b_out", [D], F32, kind="ExternalInput")
    b1 = nc.dram_tensor("b1", [HF], F32, kind="ExternalInput")
    b2 = nc.dram_tensor("b2", [D], F32, kind="ExternalInput")
    ident = nc.dram_tensor("ident", [128, 128], BF16, kind="ExternalInput")
    onesm = nc.dram_tensor("onesm", [128, 128], BF16, kind="ExternalInput")
    cpar = nc.dram_tensor("cpar", [1, 8], I32, kind="ExternalInput")
    out = nc.dram_tensor("out", [D, NG * W], F32, kind="ExternalOutput")

    bw1d = nc.dram_tensor("bw1d", [NFP, D, 256], BF16, kind="Internal")
    bw2d = nc.dram_tensor("bw2d", [HF, D], BF16, kind="Internal")
    hscr = nc.dram_tensor("hscr", [128, 2, G], BF16, kind="Internal")
    wg_d = nc.dram_tensor("wg_d", [DC, 128, DC * 128], BF16, kind="Internal")
    wo_d = nc.dram_tensor("wo_d", [DC, 128, DC * 128], BF16, kind="Internal")

    sb = nc.alloc_sbuf_tensor
    Win_bf = sb("Win_bf", [128, DC, S], BF16)
    CT_bf = sb("CT_bf", [128, SC, D], BF16)
    DT_bf = sb("DT_bf", [128, SC, D], BF16)
    A_bf = sb("A_bf", [128, SC, S], BF16)
    Bm_bf = sb("Bm_bf", [128, SC, S], BF16)
    I_bf = sb("I_bf", [128, 128], BF16)
    ones_bf = sb("ones_bf", [128, 128], BF16)

    b_in2 = sb("b_in2", [128, SC], F32)
    bg8 = sb("bg8", [128, DC], F32)
    bo8 = sb("bo8", [128, DC], F32)
    b2_8 = sb("b2_8", [128, DC], F32)
    b1_32 = sb("b1_32", [128, FC], F32)
    a1n = sb("a1n", [128, FC], F32)
    a2h = sb("a2h", [128, DC], F32)
    half_col = sb("half_col", [128, 1], F32)
    bgh8 = sb("bgh8", [128, DC], F32)
    cpar_sb = sb("cpar_sb", [1, 8], I32)
    sinit = sb("sinit", [128, 2], BF16)

    ub_ring = sb("ub_ring", [128, 2, 2 * G], BF16)
    h_ring = sb("h_ring", [128, 2, 2 * G], BF16)

    PW = max(DC * W, HF)
    F32P0 = sb("F32P0", [128, DC * W], F32)
    F32P1 = sb("F32P1", [128, PW], F32)
    F32P2 = sb("F32P2", [128, PW], F32)
    GBF = sb("GBF", [128, DC, W], BF16)
    GCVT = sb("GCVT", [128, W], F32)
    MIXT = sb("MIXT", [128, W], F32)
    XQBF = sb("XQBF", [128, DC, W], BF16)
    RBF0 = sb("RBF0", [128, PW], BF16)
    X1BF = sb("X1BF", [128, DC, W], BF16)
    YBF = sb("YBF", [128, DC, W], BF16)
    H2BF = sb("H2BF", [128, FC, H], BF16)
    LNP = sb("LNP", [128, 4, W], F32)
    USTG = sb("USTG", [128, SC, W], BF16)
    MYU = sb("MYU", [128, SC, W], BF16)
    MYH = sb("MYH", [128, SC, W], BF16)
    XSL = sb("XSL", [128, DC, W], BF16)
    BW1 = sb("BW1", [128, 2, DC, 256], BF16)
    BW2 = sb("BW2", [128, 2, 2, D], BF16)
    BWGO = sb("BWGO", [128, 2, DC, 128], BF16)

    ps_scan = [nc.alloc_psum_tensor("pss0", [128, 2], F32),
               nc.alloc_psum_tensor("pss1", [128, 2], F32)]
    PSD = [nc.alloc_psum_tensor(f"psd{i}", [128, W], F32) for i in range(4)]
    PST = [nc.alloc_psum_tensor("pst0", [128, W], F32),
           nc.alloc_psum_tensor("pst1", [128, W], F32)]

    sem_names = ["LDW", "TS", "SS", "GX", "GXQ", "GHX", "SYQ", "OUT",
                 "VDS", "SDS", "PEB2", "GB2", "WCQ"]
    sem_names += [f"MMS{i}" for i in range(6)]
    semh = {}
    semc = {n: 0 for n in sem_names}
    sem_ctx = []
    for n in sem_names:
        cm = nc.semaphore(n)
        h = cm.__enter__()
        sem_ctx.append(cm)
        semh[n] = h

    EL = {k: [] for k in ("tensor", "scalar", "vector", "sync", "gpsimd")}

    def em(engine, fn):
        EL[engine].append(fn)

    def bump(name, k=1):
        semc[name] += k
        return semc[name]

    NONE = ("VDS", 0)

    bank_rr = [0]
    bank_guard = [NONE] * 6
    bank_mms = [0] * 6

    def bank_tensor(i):
        return PSD[i] if i < 4 else PST[i - 4]

    def acquire_bank():
        i = bank_rr[0] % 4
        bank_rr[0] += 1
        return i, bank_guard[i]

    pe_weave = []
    sc_weave = []
    pe_pushed = [0]
    vds_sdsreq = {0: 0}
    dve_sds_max = [0]

    def wait_sds_req(waits):
        req = 0
        for (sn, sv) in waits:
            if sv <= 0:
                continue
            if sn == "SDS":
                req = max(req, sv)
            elif sn == "VDS":
                req = max(req, vds_sdsreq.get(sv, dve_sds_max[0]))
        return req

    def pe_push(fn, req_sds=0):
        pe_weave.append((req_sds, fn))
        pe_pushed[0] += 1

    def pe_pause(n):
        pe_weave.append(("P", n))

    def sc_push(fn):
        sc_weave.append((pe_pushed[0], semc["SDS"] + 1, fn))

    def plan_mm_group(mms, waits=(), inc_name=None, extra_inc=None,
                      per_item=2):
        chunks = [mms[i:i + per_item] for i in range(0, len(mms), per_item)]
        n = len(chunks)
        base = 0
        grp_req = wait_sds_req(waits)
        for ci, ch in enumerate(chunks):
            ww = [(sn, sv) for (sn, sv) in waits if sv > 0] if ci == 0 else []
            inc = inc_name if ci == n - 1 else None
            ei = []
            for j in range(len(ch)):
                if extra_inc is not None and base + j == extra_inc[0]:
                    ei.append((j, extra_inc[1]))

            def f(eng, ch=ch, ww=ww, inc=inc, ei=ei):
                for (sn, sv) in ww[1:]:
                    eng.wait_ge(semh[sn], sv)
                for j, (o, l, r, st, sp) in enumerate(ch):
                    m = eng.matmul(o(), l(), r(), start=st, stop=sp)
                    if j == 0 and ww:
                        m._wait_ge(semh[ww[0][0]], ww[0][1])
                    carried = False
                    for (jj, sname) in ei:
                        if jj == j:
                            m.then_inc(semh[sname], 1)
                            carried = True
                    if (not carried) and j == len(ch) - 1 and inc is not None:
                        m.then_inc(semh[inc], 1)
            pe_push(f, grp_req if ci == 0 else 0)
            base += len(ch)

    def mm_bank(mms, waits, bi, per_item=2):
        bump("MMS%d" % bi)
        bank_mms[bi] += 1
        plan_mm_group(mms, waits=waits, inc_name="MMS%d" % bi,
                      per_item=per_item)
        return ("MMS%d" % bi, bank_mms[bi])

    def dve_item(ops, waits=(), pad=False):
        if pad:
            ops = list(ops) + [lambda v: v.tensor_copy(LNP[:, 2, 0:1], half_col[:])]
        vals = [bump("VDS") for _ in ops]
        wl = [(sn, sv) for (sn, sv) in waits if sv > 0]
        dve_sds_max[0] = max(dve_sds_max[0], wait_sds_req(waits))
        for vv_ in vals:
            vds_sdsreq[vv_] = dve_sds_max[0]

        def f(v, ops=ops, wl=wl, vals=vals):
            for (sn, sv) in wl:
                v.wait_ge(semh[sn], sv)
            for i, op in enumerate(ops):
                if vals[i] - 1 > 0:
                    v.wait_ge(semh["VDS"], vals[i] - 1)
                op(v).then_inc(semh["VDS"], 1)
        em("vector", f)
        return vals

    def g_startup(g):
        c = 0
        loads = [
            (Win_bf[:], bass.AP(W_in, 0, [[S, 128], [128 * S, DC], [1, S]]), {}),
            (CT_bf[:], bass.AP(CT, 0, [[D, 128], [128 * D, SC], [1, D]]), {}),
            (DT_bf[:], bass.AP(DT, 0, [[D, 128], [128 * D, SC], [1, D]]), {}),
            (A_bf[:], bass.AP(Amat, 0, [[S, 128], [128 * S, SC], [1, S]]), {}),
            (Bm_bf[:], bass.AP(Bm, 0, [[S, 128], [128 * S, SC], [1, S]]), {}),
            (I_bf[:], ident[:], {}),
            (ones_bf[:], onesm[:], {}),
            (b_in2[:], bass.AP(b_in, 0, [[1, 128], [128, SC]]),
             dict(allow_slow_non_contiguous=True)),
            (bg8[:], bass.AP(b_gate, 0, [[1, 128], [128, DC]]),
             dict(allow_slow_non_contiguous=True)),
            (bo8[:], bass.AP(b_out, 0, [[1, 128], [128, DC]]),
             dict(allow_slow_non_contiguous=True)),
            (b2_8[:], bass.AP(b2, 0, [[1, 128], [128, DC]]),
             dict(allow_slow_non_contiguous=True)),
            (b1_32[:], bass.AP(b1, 0, [[1, 128], [128, FC]]),
             dict(allow_slow_non_contiguous=True)),
            (cpar_sb[:], cpar[:], {}),
        ]
        for (o, i, kw) in loads:
            if c > 0:
                g.wait_ge(semh["LDW"], c)
            g.dma_start(out=o, in_=i, **kw).then_inc(semh["LDW"], 16)
            c += 16

    NLDW = 13
    semc["LDW"] = 16 * NLDW
    em("gpsimd", g_startup)

    assert FC * H >= D

    def g_wcast(g):
        c = 0
        pj = 0
        for (srcm, dstm) in ((Wg, wg_d), (Wo, wo_d)):
            for k in range(DC):
                bounce = H2BF[:, 0:D // H, :] if H <= D else H2BF[:, 0:1, 0:D]
                g.wait_ge(semh["SS"], min(256 + 24 * pj, G - 64))
                pj += 1
                if c > 0:
                    g.wait_ge(semh["WCQ"], c)
                g.dma_start(out=bounce,
                            in_=bass.AP(srcm, k * 128 * D, [[D, 128], [1, D]])
                            ).then_inc(semh["WCQ"], 16)
                c += 16
                g.wait_ge(semh["WCQ"], c)
                g.dma_start(out=bass.AP(dstm, k * 128,
                                        [[DC * 128, 128], [128 * DC * 128, DC], [1, 128]]),
                            in_=bounce, allow_slow_non_contiguous=True
                            ).then_inc(semh["WCQ"], 16)
                c += 16

    def f_reg(g):
        g.wait_ge(semh["LDW"], 16 * NLDW)
        g.reg_load(build_program._roff, cpar_sb[0:1, 0:1])
    em("gpsimd", f_reg)

    stageA, stageB = F32P1, F32P2
    sgA = RBF0
    stage_guard = [NONE, NONE]
    sg_guard = [NONE]

    def boot_load(srcm, off, width, buf, pace=0):
        guard = stage_guard[buf]
        pre = semc["SYQ"]

        def f_load(s, srcm=srcm, off=off, width=width, buf=buf, guard=guard,
                   pre=pre, pace=pace):
            st = stageA if buf == 0 else stageB
            if pace > 0:
                s.wait_ge(semh["SS"], pace)
            if guard[1] > 0:
                s.wait_ge(semh[guard[0]], guard[1])
            if pre > 0:
                s.wait_ge(semh["SYQ"], pre)
            s.dma_start(out=st[:, 0:width],
                        in_=bass.AP(srcm, off, [[width, 128], [1, width]])
                        ).then_inc(semh["SYQ"], 16)
        em("sync", f_load)
        return bump("SYQ", 16)

    def boot_sign_store(width, dram_ap_fn, sbuf_src_fn, buf, lv):
        sgg = sg_guard[0]

        def f_sign(sc_, width=width, buf=buf, lv=lv, sgg=sgg):
            st = stageA if buf == 0 else stageB
            sc_.wait_ge(semh["SYQ"], lv)
            if sgg[1] > 0:
                sc_.wait_ge(semh[sgg[0]], sgg[1])
            sc_.activation(sgA[:, 0:width], st[:, 0:width], AFT.Sign
                           ).then_inc(semh["SDS"], 1)
        em("scalar", f_sign)
        sgv = bump("SDS")
        stage_guard[buf] = ("SDS", sgv)
        ghx_pre = semc["GHX"]

        def f_store(g, sgv=sgv, dram_ap_fn=dram_ap_fn, sbuf_src_fn=sbuf_src_fn,
                    ghx_pre=ghx_pre):
            g.wait_ge(semh["SDS"], sgv)
            if ghx_pre > 0:
                g.wait_ge(semh["GHX"], ghx_pre)
            g.dma_start(out=dram_ap_fn(), in_=sbuf_src_fn(sgA)
                        ).then_inc(semh["GHX"], 16)
        em("gpsimd", f_store)
        gv = bump("GHX", 16)
        sg_guard[0] = ("GHX", gv)

    for k in range(DC):
        buf = k % 2
        lv = boot_load(w1T, k * 128 * HF, HF, buf)
        boot_sign_store(
            HF,
            (lambda kk=k: bass.AP(bw1d, kk * 128 * 256,
                                  [[256, 128], [D * 256, NFP], [1, 256]])),
            (lambda sg: sg[:, 0:HF].rearrange("p (a b) -> p a b", a=NFP)),
            buf, lv)
    for k in range(FC):
        buf = k % 2
        lv = boot_load(w2T, k * 128 * D, D, buf)
        boot_sign_store(
            D,
            (lambda kk=k: bass.AP(bw2d, kk * 128 * D, [[D, 128], [1, D]])),
            (lambda sg: sg[:, 0:D]),
            buf, lv)

    def plan_alpha_consts():
        for k in range(FC):
            buf = k % 2
            lv = boot_load(w1, k * 128 * D, D, buf,
                           pace=min(28 * k, T // 2))
            vv = dve_item(
                [lambda v, kk=k, buf=buf: v.reduce_sum(
                    a1n[:, kk:kk + 1],
                    (stageA if buf == 0 else stageB)[:, 0:D],
                    mybir.AxisListType.X, apply_absolute_value=True)],
                waits=[("SYQ", lv)])
            stage_guard[buf] = ("VDS", vv[-1])
        for k in range(DC):
            buf = k % 2
            lv = boot_load(w2, k * 128 * HF, HF, buf,
                           pace=min(28 * (FC + k), T // 2))
            vv = dve_item(
                [lambda v, kk=k, buf=buf: v.reduce_sum(
                    a2h[:, kk:kk + 1],
                    (stageA if buf == 0 else stageB)[:, 0:HF],
                    mybir.AxisListType.X, apply_absolute_value=True)],
                waits=[("SYQ", lv)])
            stage_guard[buf] = ("VDS", vv[-1])

        alpha_v = semc["VDS"]
        cva = dve_item(
            [lambda v: v.tensor_scalar_mul(a1n[:], a1n[:], 1.0 / D),
             lambda v: v.tensor_scalar_mul(a2h[:], a2h[:], 1.0 / HF)],
            waits=[("VDS", alpha_v)])
        return cva[-1]

    cva_v = [0]
    cv = dve_item(
        [lambda v: v.memset(half_col[:], 0.5),
         lambda v: v.tensor_scalar_mul(bgh8[:], bg8[:], 0.5)],
        waits=[("LDW", 16 * NLDW)])
    CONSTS_V = cv[-1]
    BOOT_GHX = semc["GHX"]

    def f_sstart(sc_):
        sc_.wait_ge(semh["LDW"], 16 * NLDW)
        sc_.wait_ge(semh["VDS"], CONSTS_V)
        sc_.preload_act_table(AFT.Gelu)
        sc_.memzero(sinit[:]).then_inc(semh["SS"], 1)
    em("scalar", f_sstart)
    bump("SS")

    xsl_count = [0]
    xsl_guard = [NONE]
    xqbf_guard = [NONE]
    myh_guard = [NONE]
    gbf_guard = [NONE]
    h2_guard = [NONE]
    bw1_count = [0]
    bw1_guard = {}
    bw2_count = [0]
    bw2_guard = {}
    wgo_count = [0]
    wgo_guard = {}
    outd_count = [0]
    ext_done = {}

    PI1 = 1 if W >= 384 else 2

    def plan_uub(g_idx):
        slot = g_idx % 2
        for tt in range(4):
            n = xsl_count[0]
            guard = xsl_guard[0]
            gx_pre = 16 * n

            def f(g, g_idx=g_idx, tt=tt, guard=guard, gx_pre=gx_pre):
                if guard[1] > 0:
                    g.wait_ge(semh[guard[0]], guard[1])
                if gx_pre > 0:
                    g.wait_ge(semh["GX"], gx_pre)
                c0 = g_idx * G + tt * W
                g.dma_start(out=XSL[:],
                            in_=bass.AP(xT, c0, [[T, 128], [128 * T, DC], [1, W]])
                            ).then_inc(semh["GX"], 16)
            em("gpsimd", f)
            xsl_count[0] += 1
            gxv = bump("GX", 16)
            slab = g_idx * 4 + tt
            u_done = None
            for m in range(SC):
                bi, bg_ = acquire_bank()
                mms = []
                for k in range(DC):
                    mms.append((
                        (lambda bi=bi: bank_tensor(bi)[:, 0:W]),
                        (lambda k=k, m=m: Win_bf[:, k, m * 128:(m + 1) * 128]),
                        (lambda k=k: XSL[:, k, :]),
                        k == 0, k == DC - 1))
                mg = mm_bank(mms, [bg_, ("GX", gxv), ("LDW", 16 * NLDW)], bi, per_item=PI1)
                u_done = mg
                vv = dve_item(
                    [lambda v, bi=bi, m=m: v.tensor_scalar_add(
                        USTG[:, m, :], bank_tensor(bi)[:, 0:W],
                        b_in2[:, m:m + 1])],
                    waits=[mg], pad=True)
                bank_guard[bi] = ("VDS", vv[-1])
            xsl_guard[0] = u_done
            ustg_v = semc["VDS"]
            for j in range(SC):
                bi, bg_ = acquire_bank()
                mms = []
                for k in range(SC):
                    mms.append((
                        (lambda bi=bi: bank_tensor(bi)[:, 0:W]),
                        (lambda k=k, j=j: Bm_bf[:, k, j * 128:(j + 1) * 128]),
                        (lambda k=k: USTG[:, k, :]),
                        k == 0, k == SC - 1))
                mg = mm_bank(mms, [bg_, ("VDS", ustg_v)], bi, per_item=PI1)
                vv = dve_item(
                    [lambda v, bi=bi, j=j, slot=slot, tt=tt: v.tensor_copy(
                        ub_ring[:, j, slot * G + tt * W: slot * G + (tt + 1) * W],
                        bank_tensor(bi)[:, 0:W])],
                    waits=[mg])
                bank_guard[bi] = ("VDS", vv[-1])
        return semc["VDS"]

    def plan_layernorm(src, dst, dst_bf, src_v):
        ops = []
        for dcn in range(DC):
            ops.append(lambda v, dcn=dcn: v.tensor_copy(
                RBF0[:, dcn * W:(dcn + 1) * W], src[:, dcn * W:(dcn + 1) * W]))
        for dcn in range(DC):
            ops.append(lambda v, dcn=dcn: v.tensor_mul(
                YBF[:, dcn, :], RBF0[:, dcn * W:(dcn + 1) * W],
                RBF0[:, dcn * W:(dcn + 1) * W]))
        cast_v = dve_item(ops, waits=[("VDS", src_v)], pad=True)[-1]

        stat_g = []
        for q in range(2):
            bi = 4 + q
            bg_ = bank_guard[bi]
            mms = []
            for k in range(DC):
                if q == 0:
                    rhs_fn = (lambda k=k: RBF0[:, k * W:(k + 1) * W])
                else:
                    rhs_fn = (lambda k=k: YBF[:, k, :])
                mms.append((
                    (lambda bi=bi: bank_tensor(bi)[:, 0:W]),
                    (lambda: ones_bf[:]),
                    rhs_fn, k == 0, k == DC - 1))
            stat_g.append(mm_bank(mms, [bg_, ("VDS", cast_v)], bi, per_item=PI1))

        m = LNP[:, 0, :]
        qq = LNP[:, 1, :]
        t1 = LNP[:, 2, :]
        r = LNP[:, 3, :]
        ops = [
            lambda v: v.tensor_scalar_mul(m, PST[0][:, 0:W], 1.0 / D),
            lambda v: v.tensor_scalar_mul(qq, PST[1][:, 0:W], 1.0 / D),
            lambda v: v.tensor_mul(t1, m, m),
            lambda v: v.tensor_sub(qq, qq, t1),
            lambda v: v.tensor_scalar_add(qq, qq, 1e-5),
            lambda v: v.tensor_single_scalar(r[:].bitcast(I32), qq[:].bitcast(I32),
                                             1, ALU.logical_shift_right),
            lambda v: v.tensor_single_scalar(r[:].bitcast(I32), r[:].bitcast(I32),
                                             -1, ALU.bitwise_xor),
            lambda v: v.tensor_single_scalar(r[:].bitcast(I32), r[:].bitcast(I32),
                                             RSQRT_MAGIC + 1, ALU.add),
        ]
        for _ in range(2):
            ops += [
                lambda v: v.tensor_mul(t1, r, r),
                lambda v: v.tensor_mul(t1, t1, qq),
                lambda v: v.tensor_scalar(t1, t1, -0.5, 1.5, ALU.mult, ALU.add),
                lambda v: v.tensor_mul(r, r, t1),
            ]
        ops.append(lambda v: v.tensor_mul(m, m, r))
        for dcn in range(DC):
            sl = slice(dcn * W, (dcn + 1) * W)
            ops.append(lambda v, sl=sl: v.tensor_mul(dst[:, sl], src[:, sl], r))
            ops.append(lambda v, sl=sl: v.tensor_sub(dst[:, sl], dst[:, sl], m))
        if dst_bf is not None:
            for dcn in range(DC):
                ops.append(lambda v, dcn=dcn: v.tensor_copy(
                    dst_bf[:, dcn, :], dst[:, dcn * W:(dcn + 1) * W]))
        lvals = dve_item(ops, waits=stat_g, pad=True)
        bank_guard[4] = ("VDS", lvals[0])
        bank_guard[5] = ("VDS", lvals[1])
        return lvals[-1]

    def plan_ffn(s, th, x1_v):
        gelu_byf = {}
        for fp in range(NFP):
            buf = bw1_count[0] % 2
            guard = bw1_guard.get(buf, NONE)
            qeng, qsem = (("sync", "SYQ") if buf == 0 else ("gpsimd", "GB2"))
            q_pre = semc[qsem]

            def f_bw1(sy, fp=fp, buf=buf, guard=guard, q_pre=q_pre, qsem=qsem):
                if q_pre > 0:
                    sy.wait_ge(semh[qsem], q_pre)
                if guard[1] > 0:
                    sy.wait_ge(semh[guard[0]], guard[1])
                sy.dma_start(out=BW1[:, buf, :, :],
                             in_=bass.AP(bw1d, fp * D * 256,
                                         [[256, 128], [128 * 256, DC], [1, 256]])
                             ).then_inc(semh[qsem], 16)
            em(qeng, f_bw1)
            bw1_count[0] += 1
            slab_v = (qsem, bump(qsem, 16))

            for hh in range(2):
                f = 2 * fp + hh
                bi, bg_ = acquire_bank()
                mms = []
                for k in range(DC):
                    mms.append((
                        (lambda bi=bi: bank_tensor(bi)[:, 0:H]),
                        (lambda k=k, hh=hh, buf=buf: BW1[:, buf, k, hh * 128:(hh + 1) * 128]),
                        (lambda k=k, th=th: X1BF[:, k, th * H:(th + 1) * H]),
                        k == 0, k == DC - 1))
                mg = mm_bank(mms, [bg_, slab_v, ("VDS", x1_v)], bi)
                if hh == 1:
                    bw1_guard[buf] = mg

                h2g = h2_guard[0]

                def fsc(sc_, bi=bi, f=f, mg=mg, h2g=h2g):
                    sc_.wait_ge(semh[mg[0]], mg[1])
                    if h2g[1] > 0:
                        sc_.wait_ge(semh[h2g[0]], h2g[1])
                    sc_.wait_ge(semh["VDS"], cva_v[0])
                    sc_.activation(H2BF[:, f, :], bank_tensor(bi)[:, 0:H],
                                   AFT.Gelu, bias=b1_32[:, f:f + 1],
                                   scale=a1n[:, f:f + 1]).then_inc(semh["SDS"], 1)
                sc_push(fsc)
                gelu_v = bump("SDS")
                bank_guard[bi] = ("SDS", gelu_v)
                gelu_byf[f] = gelu_v
        h2_v = ("SDS", semc["SDS"])

        for pass_ in range(2):
            dcs = [pass_ * 4 + i for i in range(4)]
            bank_waits = [bank_guard[i] for i in range(4)]
            for f in range(FC):
                fp, half = f // 2, f % 2
                if half == 0:
                    buf = bw2_count[0] % 2
                    guard = bw2_guard.get(buf, NONE)
                    qeng, qsem = (("gpsimd", "GB2") if buf == 0 else ("sync", "SYQ"))
                    q_pre = semc[qsem]

                    def f_bw2(g, fp=fp, buf=buf, guard=guard, q_pre=q_pre, qsem=qsem):
                        if q_pre > 0:
                            g.wait_ge(semh[qsem], q_pre)
                        if guard[1] > 0:
                            g.wait_ge(semh[guard[0]], guard[1])
                        g.dma_start(out=BW2[:, buf, :, :],
                                    in_=bass.AP(bw2d, fp * 256 * D,
                                                [[D, 128], [128 * D, 2], [1, D]])
                                    ).then_inc(semh[qsem], 16)
                    em(qeng, f_bw2)
                    bw2_count[0] += 1
                    slab_v = (qsem, bump(qsem, 16))
                    cur_bw2 = (buf, slab_v)
                buf, slab_v = cur_bw2

                mms = []
                for i, dcn in enumerate(dcs):
                    mms.append((
                        (lambda i=i: bank_tensor(i)[:, 0:H]),
                        (lambda dcn=dcn, buf=buf, half=half: BW2[:, buf, half, dcn * 128:(dcn + 1) * 128]),
                        (lambda f=f, th=th: H2BF[:, f, :]),
                        f == 0, f == FC - 1))
                ww = [slab_v, ("SDS", gelu_byf[f])]
                if f == 0:
                    ww = ww + bank_waits
                if f < FC - 1:
                    if half == 1:
                        plan_mm_group(mms, waits=ww, inc_name=None,
                                      extra_inc=(3, "PEB2"))
                        bw2_guard[buf] = ("PEB2", bump("PEB2"))
                    else:
                        plan_mm_group(mms, waits=ww, inc_name=None)
                else:
                    for i in range(4):
                        bump("MMS%d" % i)
                        bank_mms[i] += 1
                    req = wait_sds_req(ww)

                    def f_last(eng, mms=mms, ww=ww):
                        wl = [(a, b) for (a, b) in ww if b > 0]
                        for (sn, sv) in wl[1:]:
                            eng.wait_ge(semh[sn], sv)
                        for i, (o, l, r, st, sp) in enumerate(mms):
                            m = eng.matmul(o(), l(), r(), start=st, stop=sp)
                            if i == 0 and wl:
                                m._wait_ge(semh[wl[0][0]], wl[0][1])
                            m.then_inc(semh["MMS%d" % i], 1)
                    pe_push(f_last, req)
                    bw2_guard[buf] = ("MMS3", bank_mms[3])
            pass_g = [("MMS%d" % i, bank_mms[i]) for i in range(4)]
            if pass_ == 1:
                h2_guard[0] = ("MMS3", bank_mms[3])

            for i, dcn in enumerate(dcs):
                mg = pass_g[i]
                vv = dve_item(
                    [lambda v, i=i, dcn=dcn, th=th: v.tensor_scalar(
                        F32P1[:, dcn * W + th * H: dcn * W + (th + 1) * H],
                        bank_tensor(i)[:, 0:H],
                        a2h[:, dcn:dcn + 1], b2_8[:, dcn:dcn + 1],
                        ALU.mult, ALU.add),
                     lambda v, dcn=dcn, th=th: v.tensor_add(
                        F32P1[:, dcn * W + th * H: dcn * W + (th + 1) * H],
                        F32P1[:, dcn * W + th * H: dcn * W + (th + 1) * H],
                        F32P2[:, dcn * W + th * H: dcn * W + (th + 1) * H])],
                    waits=[mg])
                bank_guard[i] = ("VDS", vv[0])

    def plan_ds_io(s):
        slot = s % 2
        s_endv = 1 + (s + 1) * G
        ghx_pre = semc["GHX"]

        def f_ext(g, slot=slot, s_endv=s_endv, ghx_pre=ghx_pre):
            g.wait_ge(semh["SS"], s_endv)
            if ghx_pre > 0:
                g.wait_ge(semh["GHX"], ghx_pre)
            g.dma_start(out=hscr[:], in_=h_ring[:, :, slot * G:(slot + 1) * G]
                        ).then_inc(semh["GHX"], 16)
        em("gpsimd", f_ext)
        v1 = bump("GHX", 16)
        ext_done[s] = v1
        mhg = myh_guard[0]

        def f_ext2(g, v1=v1, mhg=mhg):
            g.wait_ge(semh["GHX"], v1)
            if mhg[1] > 0:
                g.wait_ge(semh[mhg[0]], mhg[1])
            g.dma_start(out=MYH[:],
                        in_=bass.AP(hscr, build_program._roff,
                                    [[2 * G, 128], [G, 2], [1, W]])
                        ).then_inc(semh["GHX"], 16)
        em("gpsimd", f_ext2)
        myh_v = bump("GHX", 16)

        syq_pre = semc["SYQ"]
        need_out = 16 * outd_count[0]

        def f_xq32(sy, s=s, need_out=need_out, syq_pre=syq_pre):
            if s == 0:
                sy.wait_ge(semh["GHX"], BOOT_GHX)
                sy.wait_ge(semh["WCQ"], WCAST_V)
            if need_out > 0:
                sy.wait_ge(semh["OUT"], need_out)
            if syq_pre > 0:
                sy.wait_ge(semh["SYQ"], syq_pre)
            sy.dma_start(out=F32P0[:],
                         in_=bass.AP(xqT, s * W, [[NG * W, 128], [128 * NG * W, DC], [1, W]])
                         ).then_inc(semh["SYQ"], 16)
        em("sync", f_xq32)
        xq32_v = bump("SYQ", 16)

        xqg = xqbf_guard[0]
        gxq_pre = semc["GXQ"]

        def f_xqbf(g, s=s, xqg=xqg, gxq_pre=gxq_pre):
            if xqg[1] > 0:
                g.wait_ge(semh[xqg[0]], xqg[1])
            if gxq_pre > 0:
                g.wait_ge(semh["GXQ"], gxq_pre)
            g.dma_start(out=XQBF[:],
                        in_=bass.AP(xqT, s * W, [[NG * W, 128], [128 * NG * W, DC], [1, W]])
                        ).then_inc(semh["GXQ"], 16)
        em("gpsimd", f_xqbf)
        xqbf_v = bump("GXQ", 16)
        return myh_v, xq32_v, xqbf_v

    def plan_downstream(s, io):
        myh_v, xq32_v, xqbf_v = io

        for m in range(SC):
            bi, bg_ = acquire_bank()
            mms = []
            for k in range(DC):
                mms.append((
                    (lambda bi=bi: bank_tensor(bi)[:, 0:W]),
                    (lambda k=k, m=m: Win_bf[:, k, m * 128:(m + 1) * 128]),
                    (lambda k=k: XQBF[:, k, :]),
                    k == 0, k == DC - 1))
            mg = mm_bank(mms, [bg_, ("GXQ", xqbf_v)], bi, per_item=PI1)
            vv = dve_item(
                [lambda v, bi=bi, m=m: v.tensor_scalar_add(
                    MYU[:, m, :], bank_tensor(bi)[:, 0:W], b_in2[:, m:m + 1])],
                waits=[mg], pad=True)
            bank_guard[bi] = ("VDS", vv[-1])
        myu_v = semc["VDS"]

        sig_last = 0
        d2_last = NONE
        for e in range(DC):
            buf = wgo_count[0] % 2
            guard = wgo_guard.get(buf, NONE)
            syq_pre2 = semc["SYQ"]

            def f_wg(sy, e=e, buf=buf, guard=guard, syq_pre2=syq_pre2):
                if syq_pre2 > 0:
                    sy.wait_ge(semh["SYQ"], syq_pre2)
                if guard[1] > 0:
                    sy.wait_ge(semh[guard[0]], guard[1])
                sy.dma_start(out=BWGO[:, buf, :, :],
                             in_=bass.AP(wg_d, e * 128 * DC * 128,
                                         [[DC * 128, 128], [1, DC * 128]])
                             ).then_inc(semh["SYQ"], 16)
            em("sync", f_wg)
            wgo_count[0] += 1
            slab_v = bump("SYQ", 16)

            bi, bg_ = acquire_bank()
            mms = []
            for k in range(DC):
                mms.append((
                    (lambda bi=bi: bank_tensor(bi)[:, 0:W]),
                    (lambda k=k, buf=buf: BWGO[:, buf, k, :]),
                    (lambda k=k: XQBF[:, k, :]),
                    k == 0, k == DC - 1))
            mg = mm_bank(mms, [bg_, ("GXQ", xqbf_v), ("SYQ", slab_v)], bi, per_item=PI1)
            wgo_guard[buf] = mg
            d2_last = mg

            gbg = gbf_guard[0]

            def fsc(sc_, bi=bi, e=e, mg=mg, gbg=gbg):
                sc_.wait_ge(semh[mg[0]], mg[1])
                if gbg[1] > 0:
                    sc_.wait_ge(semh[gbg[0]], gbg[1])
                sc_.activation(GBF[:, e, :], bank_tensor(bi)[:, 0:W],
                               AFT.Tanh, bias=bgh8[:, e:e + 1],
                               scale=half_col[:]).then_inc(semh["SDS"], 1)
            sc_push(fsc)
            sv = bump("SDS")
            sig_last = sv
            bank_guard[bi] = ("SDS", sv)
        xqbf_guard[0] = d2_last
        g_done = sig_last

        d3_last = NONE
        mix_last = NONE
        for e in range(DC):
            bi, bg_ = acquire_bank()
            mms = []
            for k in range(SC):
                mms.append((
                    (lambda bi=bi: bank_tensor(bi)[:, 0:W]),
                    (lambda k=k, e=e: CT_bf[:, k, e * 128:(e + 1) * 128]),
                    (lambda k=k: MYH[:, k, :]),
                    k == 0, False))
            for k in range(SC):
                mms.append((
                    (lambda bi=bi: bank_tensor(bi)[:, 0:W]),
                    (lambda k=k, e=e: DT_bf[:, k, e * 128:(e + 1) * 128]),
                    (lambda k=k: MYU[:, k, :]),
                    False, k == SC - 1))
            mg = mm_bank(mms, [bg_, ("GHX", myh_v), ("VDS", myu_v)], bi, per_item=PI1)
            d3_last = mg
            vv = dve_item(
                [lambda v, e=e: v.tensor_scalar(
                    GCVT[:], GBF[:, e, :], 0.5, 0.5, ALU.mult, ALU.add),
                 lambda v, bi=bi, e=e: v.tensor_sub(
                    MIXT[:], bank_tensor(bi)[:, 0:W], F32P0[:, e * W:(e + 1) * W]),
                 lambda v: v.tensor_mul(MIXT[:], MIXT[:], GCVT[:]),
                 lambda v, e=e: v.tensor_add(
                    YBF[:, e, :], MIXT[:], F32P0[:, e * W:(e + 1) * W])],
                waits=[mg, ("SYQ", xq32_v), ("SDS", g_done)], pad=True)
            bank_guard[bi] = ("VDS", vv[1])
            mix_last = ("VDS", vv[-1])
        myh_guard[0] = d3_last
        gbf_guard[0] = mix_last
        ybf_v = mix_last[1]

        for e2 in range(DC):
            buf = wgo_count[0] % 2
            guard = wgo_guard.get(buf, NONE)
            syq_pre3 = semc["SYQ"]

            def f_wo(sy, e2=e2, buf=buf, guard=guard, syq_pre3=syq_pre3):
                if syq_pre3 > 0:
                    sy.wait_ge(semh["SYQ"], syq_pre3)
                if guard[1] > 0:
                    sy.wait_ge(semh[guard[0]], guard[1])
                sy.dma_start(out=BWGO[:, buf, :, :],
                             in_=bass.AP(wo_d, e2 * 128 * DC * 128,
                                         [[DC * 128, 128], [1, DC * 128]])
                             ).then_inc(semh["SYQ"], 16)
            em("sync", f_wo)
            wgo_count[0] += 1
            slab_v = bump("SYQ", 16)

            bi, bg_ = acquire_bank()
            mms = []
            for k in range(DC):
                mms.append((
                    (lambda bi=bi: bank_tensor(bi)[:, 0:W]),
                    (lambda k=k, buf=buf: BWGO[:, buf, k, :]),
                    (lambda k=k: YBF[:, k, :]),
                    k == 0, k == DC - 1))
            mg = mm_bank(mms, [bg_, ("VDS", ybf_v), ("SYQ", slab_v)], bi, per_item=PI1)
            wgo_guard[buf] = mg
            vv = dve_item(
                [lambda v, bi=bi, e2=e2: v.tensor_scalar_add(
                    F32P1[:, e2 * W:(e2 + 1) * W], bank_tensor(bi)[:, 0:W],
                    bo8[:, e2:e2 + 1]),
                 lambda v, e2=e2: v.tensor_add(
                    F32P1[:, e2 * W:(e2 + 1) * W], F32P1[:, e2 * W:(e2 + 1) * W],
                    F32P0[:, e2 * W:(e2 + 1) * W]),
                 lambda v, e2=e2: v.tensor_add(
                    F32P1[:, e2 * W:(e2 + 1) * W], F32P1[:, e2 * W:(e2 + 1) * W],
                    F32P0[:, e2 * W:(e2 + 1) * W])],
                waits=[mg])
            bank_guard[bi] = ("VDS", vv[0])
        res1_v = semc["VDS"]

        pe_pause(12)
        x1_v = plan_layernorm(F32P1, F32P2, X1BF, res1_v)
        pe_pause(40)
        for th in range(2):
            plan_ffn(s, th, x1_v)
        res2_v = semc["VDS"]
        pe_pause(44)
        x2_v = plan_layernorm(F32P1, F32P0, None, res2_v)

        out_pre = 16 * outd_count[0]

        def f_out(sy, s=s, x2_v=x2_v, out_pre=out_pre):
            sy.wait_ge(semh["VDS"], x2_v)
            if out_pre > 0:
                sy.wait_ge(semh["OUT"], out_pre)
            sy.dma_start(out=bass.AP(out, s * W, [[NG * W, 128], [128 * NG * W, DC], [1, W]]),
                         in_=F32P0[:]).then_inc(semh["OUT"], 16)
        em("sync", f_out)
        outd_count[0] += 1
        bump("OUT", 16)

    def plan_scan_step(g_idx, tl, ub_ready_v):
        t = g_idx * G + tl
        slot = g_idx % 2
        pslot = (g_idx - 1) % 2
        p = ps_scan[t % 2]
        ext_v = ext_done.get(g_idx - 2, 0) if tl == 0 else 0

        def f(eng, g_idx=g_idx, tl=tl, t=t, slot=slot, pslot=pslot, p=p,
              ub_ready_v=ub_ready_v, ext_v=ext_v):
            if ext_v > 0:
                eng.wait_ge(semh["GHX"], ext_v)
            m0 = eng.matmul(p[:, :], I_bf[:],
                            ub_ring[:, :, slot * G + tl: slot * G + tl + 1],
                            start=True, stop=False)
            if tl == 0 and ub_ready_v is not None:
                m0._wait_ge(semh["VDS"], ub_ready_v)
            if t == 0:
                rhs0, rhs1 = sinit[:, 0:1], sinit[:, 1:2]
            elif tl == 0:
                rhs0 = h_ring[:, 0, pslot * G + G - 1: pslot * G + G]
                rhs1 = h_ring[:, 1, pslot * G + G - 1: pslot * G + G]
            else:
                rhs0 = h_ring[:, 0, slot * G + tl - 1: slot * G + tl]
                rhs1 = h_ring[:, 1, slot * G + tl - 1: slot * G + tl]
            m1 = eng.matmul(p[:, 0:1], A_bf[:, 0, 0:128], rhs0, start=False, stop=False)
            m1._wait_ge(semh["SS"], t + 1)
            eng.matmul(p[:, 0:1], A_bf[:, 1, 0:128], rhs1, start=False, stop=False)
            eng.matmul(p[:, 1:2], A_bf[:, 0, 128:256], rhs0, start=False, stop=False)
            eng.matmul(p[:, 1:2], A_bf[:, 1, 128:256], rhs1, start=False,
                       stop=True).then_inc(semh["TS"], 1)
        bump("TS")

        def fsc(sc_, t=t, slot=slot, tl=tl, p=p):
            sc_.wait_ge(semh["TS"], t + 1)
            sc_.activation(h_ring[:, :, slot * G + tl: slot * G + tl + 1],
                           p[:, :], AFT.Tanh).then_inc(semh["SS"], 1)
        bump("SS")
        return f, fsc

    SC_PACE = 6
    pe_popped = [0]
    sds_placed = [semc["SDS"]]

    def sc_place_one():
        need_pe, sds_val, fn = sc_weave.pop(0)
        assert need_pe <= pe_popped[0], "sc item placed before its producer"
        EL["scalar"].append(fn)
        sds_placed[0] = sds_val

    def pe_drain_one():
        req, fn = pe_weave[0]
        if req == "P":
            if fn <= 1:
                pe_weave.pop(0)
            else:
                pe_weave[0] = ("P", fn - 1)
            return
        while req > sds_placed[0]:
            assert sc_weave, "PE item needs SDS that no queued sc item makes"
            sc_place_one()
        pe_weave.pop(0)
        em("tensor", fn)
        pe_popped[0] += 1

    def sc_try_drain():
        if sc_weave and sc_weave[0][0] <= pe_popped[0]:
            sc_place_one()

    ub_ready = {}
    ub_ready[0] = plan_uub(0)
    while pe_weave:
        pe_drain_one()
    for g_idx in range(NG):
        if g_idx >= 1:
            ds_io = plan_ds_io(g_idx - 1)
        if g_idx + 1 < NG:
            ub_ready[g_idx + 1] = plan_uub(g_idx + 1)
        if g_idx == 0:
            em("gpsimd", g_wcast)
            WCAST_V = bump("WCQ", 16 * 2 * DC * 2)
            cva_v[0] = plan_alpha_consts()
        if g_idx >= 1:
            plan_downstream(g_idx - 1, ds_io)
        for tl in range(G):
            f, fsc = plan_scan_step(g_idx, tl, ub_ready.get(g_idx))
            em("tensor", f)
            EL["scalar"].append(fsc)
            if pe_weave:
                pe_drain_one()
            if tl % SC_PACE == 0:
                sc_try_drain()
        while pe_weave:
            if pe_weave[0][0] == "P":
                pe_weave.pop(0)
                continue
            pe_drain_one()
        while sc_weave:
            sc_place_one()

    plan_downstream(NG - 1, plan_ds_io(NG - 1))
    while pe_weave:
        if pe_weave[0][0] == "P":
            pe_weave.pop(0)
            continue
        pe_drain_one()
    while sc_weave:
        sc_place_one()

    final_out = semc["OUT"]

    def f_fin(sy, final_out=final_out):
        sy.wait_ge(semh["OUT"], final_out)
    em("sync", f_fin)

    build_program._roff = nc.gpsimd.alloc_register("roff")
    with nc.Block() as block:

        @block.tensor
        def _(tensor):
            for f in EL["tensor"]:
                f(tensor)

        @block.scalar
        def _(scalar):
            for f in EL["scalar"]:
                f(scalar)

        @block.vector
        def _(vector):
            for f in EL["vector"]:
                f(vector)

        @block.sync
        def _(sync):
            for f in EL["sync"]:
                f(sync)

        @block.gpsimd
        def _(gpsimd):
            for f in EL["gpsimd"]:
                f(gpsimd)

    for cm in reversed(sem_ctx):
        cm.__exit__(None, None, None)
    return nc


def make_in_maps(inputs, W=512, NG=8):
    G = 4 * W
    x = np.asarray(inputs["L0_attn_x"], np.float32)
    B = x.shape[0]
    wts = {
        "W_in": np.ascontiguousarray(np.asarray(inputs["L0_attn_W_in"], dtype=np.float32)),
        "Bm": np.ascontiguousarray(np.asarray(inputs["L0_attn_Bm"], dtype=np.float32)),
        "Amat": np.ascontiguousarray(np.asarray(inputs["L0_attn_A"], dtype=np.float32)),
        "CT": np.ascontiguousarray(np.asarray(inputs["L0_attn_C"], dtype=np.float32).T),
        "DT": np.ascontiguousarray(np.asarray(inputs["L0_attn_D"], dtype=np.float32).T),
        "Wg": np.ascontiguousarray(np.asarray(inputs["L0_attn_W_gate"], dtype=np.float32)),
        "Wo": np.ascontiguousarray(np.asarray(inputs["L0_attn_W_out"], dtype=np.float32)),
        "w1": np.ascontiguousarray(np.asarray(inputs["L0_ffn_w1"], dtype=np.float32)),
        "w1T": np.ascontiguousarray(np.asarray(inputs["L0_ffn_w1"], dtype=np.float32).T),
        "w2": np.ascontiguousarray(np.asarray(inputs["L0_ffn_w2"], dtype=np.float32)),
        "w2T": np.ascontiguousarray(np.asarray(inputs["L0_ffn_w2"], dtype=np.float32).T),
        "b_in": np.ascontiguousarray(np.asarray(inputs["L0_attn_b_in"], dtype=np.float32)),
        "b_gate": np.ascontiguousarray(np.asarray(inputs["L0_attn_b_gate"], dtype=np.float32)),
        "b_out": np.ascontiguousarray(np.asarray(inputs["L0_attn_b_out"], dtype=np.float32)),
        "b1": np.ascontiguousarray(np.asarray(inputs["L0_ffn_b1"], dtype=np.float32)),
        "b2": np.ascontiguousarray(np.asarray(inputs["L0_ffn_b2"], dtype=np.float32)),
        "ident": np.eye(128, dtype=np.float32).astype(BF),
        "onesm": np.ones((128, 128), np.float32).astype(BF),
    }
    in_maps = []
    for c in range(NCORE):
        b = c // (NCORE // B)
        r = c % (NCORE // B)
        xb = x[b]
        xTb = np.ascontiguousarray(xb.T)
        idx = np.concatenate([np.arange(g * G + r * W, g * G + (r + 1) * W)
                              for g in range(NG)])
        xq = np.ascontiguousarray(xb[idx].T)
        cp = np.zeros((1, 8), np.int32)
        cp[0, 0] = r * W
        m = dict(wts)
        m["xT"] = xTb
        m["xqT"] = xq
        m["cpar"] = cp
        in_maps.append(m)
    return in_maps


def assemble(results, B, T, W=512, NG=8):
    G = 4 * W
    R = NCORE // B
    outx = np.empty((B, T, D), np.float32)
    for c in range(NCORE):
        b = c // R
        r = c % R
        oc = results[c]["out"]
        for g in range(NG):
            outx[b, g * G + r * W:g * G + (r + 1) * W, :] = oc[:, g * W:(g + 1) * W].T
    return outx


def kernel(**inputs):
    W, NG = 512, 8
    T = NG * 4 * W
    x = np.asarray(inputs["L0_attn_x"], np.float32)
    B = x.shape[0]
    nc = build_program(W=W, NG=NG)
    in_maps = make_in_maps(inputs, W=W, NG=NG)
    res = run_bass_kernel_spmd(nc, in_maps, core_ids=list(range(NCORE)))
    return assemble(res.results, B, T, W=W, NG=NG)
